# version 09
# baseline (speedup 1.0000x reference)
import math
from contextlib import ExitStack

import numpy as np
import ml_dtypes

import concourse.bass as bass
import concourse.mybir as mybir
from concourse.bass_utils import run_bass_kernel_spmd
from concourse.alu_op_type import AluOpType as Op

F32 = mybir.dt.float32
BF16 = mybir.dt.bfloat16
FP16 = mybir.dt.float16
I32 = mybir.dt.int32
AF = mybir.ActivationFunctionType
AX = mybir.AxisListType

L_INIT = 0.8 - 0.6 * math.exp(-0.3)
LNL = math.log(1.0 - L_INIT)
SEQ, DM, DH = 8192, 2048, 256
CHUNK = 512
NCH = SEQ // CHUNK
TPC = CHUNK // 128
NQT = SEQ // 128
TSL = SEQ // 8
GUF = 11264
IFF = 5632
DQS = 1024

BF = ml_dtypes.bfloat16


class Ctr:
    def __init__(self, ctx, nc, name, step):
        self.h = ctx.enter_context(nc.semaphore(name))
        self.v = 0
        self.step = step

    def inc(self, inst, n=None):
        n = self.step if n is None else n
        inst.then_inc(self.h, n)
        self.v += n
        return self.v


class Builder:
    def __init__(self, debug=False):
        self.debug = debug
        self.dbg_tile = 11
        self.nc = bass.Bass("TRN2", target_bir_lowering=False)

    def declare(self):
        nc = self.nc
        d = lambda n, s, t, k="ExternalInput": nc.dram_tensor(n, s, t, kind=k)
        self.x_full = d("x_full", [SEQ, DM], F32)
        self.x_sl = d("x_sl", [TSL, DM], F32)
        self.wqT = d("wqT", [DM, 2 * DH], F32)
        self.wkT = d("wkT", [DM, 2 * DH], F32)
        self.wvT = d("wvT", [DM, DH], F32)
        self.woT = d("woT", [DM, DM], F32)
        self.lv = d("lv", [1, 32], F32)
        self.gup = d("gup", [256, GUF], I32)
        self.gus = d("gus", [256, GUF], FP16)
        self.dnp = d("dnp", [704, DM], I32)
        self.dns = d("dns", [704, DM], FP16)
        self.ident = d("ident", [128, 128], BF16)
        self.mask01 = d("mask01", [128, 128], BF16)
        self.bdiag = d("bdiag", [128, 4], F32)
        self.bdiagT = d("bdiagT", [4, 128], F32)
        self.onesc = d("onesc", [128, 1], BF16)
        self.shp = d("shp", [128, 1], I32)
        self.co = d("co", [1, 1], I32)
        self.out_c = d("out_c", [TSL, DM], F32, "ExternalOutput")
        self.wqT_b = d("wqT_b", [DM, 2 * DH], BF16, "Internal")
        self.wkT_b = d("wkT_b", [DM, 2 * DH], BF16, "Internal")
        self.wvT_b = d("wvT_b", [DM, DH], BF16, "Internal")
        self.woT_b = d("woT_b", [DM, DM], BF16, "Internal")
        self.xnT_all = d("xnT_all", [DM, SEQ], BF16, "Internal")
        self.h_scr = d("h_scr", [IFF, TSL], BF16, "Internal")
        self.cc_gu_i = [d("cc_gu_i%d" % c, [256, 512], BF16, "Internal")
                        for c in range(22)]
        self.cc_gu_o = [d("cc_gu_o%d" % c, [DM, 512], BF16, "Internal")
                        for c in range(22)]
        self.cc_dn_i = [d("cc_dn_i%d" % c, [704, 256], BF16, "Internal")
                        for c in range(8)]
        self.cc_dn_o = [d("cc_dn_o%d" % c, [IFF, 256], BF16, "Internal")
                        for c in range(8)]
        self.cc_o_i = [d("cc_o_i%d" % c, [DM, 128], BF16, "Internal")
                       for c in range(8)]
        self.cc_o_o = [d("cc_o_o%d" % c, [DM, 128], BF16, "Internal")
                       for c in range(8)]
        self.lam_d = d("lam_d", [1, 1], F32, "Internal")
        self.mb_d = d("mb_d", [1, 8], F32, "Internal")
        if self.debug:
            self.dbg_kt = d("dbg_kt", [128, 4 * SEQ], BF16, "ExternalOutput")
            self.dbg_v = d("dbg_v", [128, NQT * DH], BF16, "ExternalOutput")
            self.dbg_e1 = d("dbg_e1", [128, SEQ], BF16, "ExternalOutput")
            self.dbg_e2 = d("dbg_e2", [128, SEQ], BF16, "ExternalOutput")
            self.dbg_sm = d("dbg_sm", [128, 8], F32, "ExternalOutput")
            self.dbg_xnt = d("dbg_xnt", [128, 16 * CHUNK], BF16, "ExternalOutput")
            self.dbg_on = d("dbg_on", [128, 256], BF16, "ExternalOutput")
            self.dbg_rc1 = d("dbg_rc1", [128, 18], F32, "ExternalOutput")
            self.dbg_rc2 = d("dbg_rc2", [128, 18], F32, "ExternalOutput")
            self.dbg_mbb = d("dbg_mbb", [128, 8], F32, "ExternalOutput")
            self.dbg_snap = d("dbg_snap", [128, 40], F32, "ExternalOutput")
            self.dbg_os = d("dbg_os", [128, 256], F32, "ExternalOutput")
            self.dbg_rbs = d("dbg_rbs", [128, 256], F32, "ExternalOutput")
            self.dbg_g = d("dbg_g", [128, 2048], BF16, "ExternalOutput")
            self.dbg_s = d("dbg_s", [128, 1024], F32, "ExternalOutput")
            self.dbg_ot = d("dbg_ot", [128, 16 * 1024], BF16, "ExternalOutput")
            self.dbg_wgu = d("dbg_wgu", [128, 1024], BF16, "ExternalOutput")
            self.dbg_y = d("dbg_y", [128, 2048], F32, "ExternalOutput")
            self.dbg_y2 = d("dbg_y2", [128, 1024], F32, "ExternalOutput")
            self.dbg_y3 = d("dbg_y3", [128, 1024], F32, "ExternalOutput")
            self.dbg_wdn = d("dbg_wdn", [5632, 256], BF16, "ExternalOutput")

    def sems(self, ctx):
        nc = self.nc
        self.qS = Ctr(ctx, nc, "qS", 16)
        self.qG = Ctr(ctx, nc, "qG", 16)
        self.sT = Ctr(ctx, nc, "sT", 1)
        self.sV = Ctr(ctx, nc, "sV", 1)
        self.sS = Ctr(ctx, nc, "sS", 1)
        self.sC = Ctr(ctx, nc, "sC", 1)
        self.qXT = Ctr(ctx, nc, "qXT", 16)
        self.qSL = Ctr(ctx, nc, "qSL", 16)
        self.qXS = Ctr(ctx, nc, "qXS", 16)
        self.qOS = Ctr(ctx, nc, "qOS", 16)
        self.qH = Ctr(ctx, nc, "qH", 16)
        self.qW = Ctr(ctx, nc, "qW", 16)
        self.qOT = Ctr(ctx, nc, "qOT", 16)
        self.qWo = Ctr(ctx, nc, "qWo", 16)
        self.qX = Ctr(ctx, nc, "qX", 16)
        self.qU = Ctr(ctx, nc, "qU", 16)
        self.qMB = Ctr(ctx, nc, "qMB", 16)

    def _all_sems(self):
        return [("T", self.sT), ("V", self.sV), ("S", self.sS),
                ("qS", self.qS), ("qG", self.qG), ("qXT", self.qXT),
                ("qSL", self.qSL), ("qXS", self.qXS), ("qOS", self.qOS),
                ("qH", self.qH), ("qW", self.qW), ("qOT", self.qOT),
                ("qWo", self.qWo), ("qX", self.qX), ("qU", self.qU),
                ("qMB", self.qMB)]

    def snap(self):
        return {k: c.v for k, c in self._all_sems()}

    def wait_snap(self, e, sn):
        for k, sem in self._all_sems():
            if sn[k] > 0:
                e.wait_ge(sem.h, sn[k])

    def fv(self, v, big=False):
        v.tensor_copy(self.fnc[:, 0:128], self.fnc[:, 128:256])
        v.tensor_copy(self.fnc[:, 0:128], self.fnc[:, 128:256])
        v.tensor_copy(self.fnc[:, 0:128], self.fnc[:, 128:256])
        if big:
            for _ in range(6):
                v.tensor_copy(self.fnc[:, 0:128], self.fnc[:, 128:256])

    def fs(self, sc):
        sc.copy(self.fnc[0:1, 0:128], self.fnc[0:1, 128:256])

    def fp(self, t, bank11):
        t.matmul(bank11, self.ones1[:], self.ones1[:], start=True, stop=True,
                 skip_group_check=True)

    def wt(self, e, mk):
        kind, v = mk
        if v <= 0:
            return
        sem = {"T": self.sT, "V": self.sV, "S": self.sS}[kind]
        e.wait_ge(sem.h, v)

    def alloc1(self, C):
        nc = self.nc
        sb = lambda n, s, t: C.enter_context(nc.sbuf_tensor(n, s, t))
        ps = lambda n, s, t: C.enter_context(nc.psum_tensor(n, s, t))
        self.KT = sb("KT", [128, 4 * SEQ], BF16)
        self.Vt = sb("Vt", [128, NQT * DH], BF16)
        self.E1 = sb("E1", [128, SEQ], BF16)
        self.E2 = sb("E2", [128, SEQ], BF16)
        self.xnT = sb("xnT", [128, 16 * CHUNK], BF16)
        self.xb = sb("xb", [128, DM], F32)
        self.xn_tok = sb("xn_tok", [128, DM], BF16)
        self.qTl = sb("qTl", [128, 4 * CHUNK], BF16)
        self.vts = sb("vts", [128, 2 * CHUNK], BF16)
        self.wsl = [sb("wsl0", [128, 16 * 128], BF16), sb("wsl1", [128, 16 * 128], BF16)]
        self.tsb1 = sb("tsb1", [128, 512], F32)
        self.tsb2 = sb("tsb2", [128, 512], F32)
        self.tsbP = sb("tsbP", [128, 512], BF16)
        self.PT = [sb("PT0", [128, 512], BF16), sb("PT1", [128, 512], BF16)]
        self.o_s = sb("o_s", [128, 256], F32)
        self.sqo = sb("sqo", [128, 256], F32)
        self.rbs = sb("rbs", [128, 256], F32)
        self.o_n = sb("o_n", [128, 256], BF16)
        self.sqn = sb("sqn", [128, 1024], BF16)
        self.stats = sb("stats", [128, 26], F32)
        self.mrs = sb("mrs", [128, 2], F32)
        self.racc = [sb("racc1", [128, 18], F32), sb("racc2", [128, 18], F32)]
        self.sm = sb("sm", [128, 8], F32)
        self.Mbb = sb("Mbb", [128, 8], F32)
        self.lamb = sb("lamb", [128, 1], F32)
        self.epsLN = sb("epsLN", [128, 1], F32)
        self.eps6 = sb("eps6", [128, 1], F32)
        self.lnl = sb("lnl", [128, 1], F32)
        self.idt = sb("idt", [128, 128], BF16)
        self.msk = sb("msk", [128, 128], BF16)
        self.bd = sb("bd", [128, 4], F32)
        self.bdT = sb("bdT", [4, 128], F32)
        self.ones1 = sb("ones1", [128, 1], BF16)
        self.lvs = sb("lvs", [1, 32], F32)
        self.lsc = sb("lsc", [1, 16], F32)
        self.nrow = sb("nrow", [1, 1024], F32)
        self.nks = sb("nks", [1, 32], F32)
        self.Mb1 = sb("Mb1", [1, 8], F32)
        self.shp_s = sb("shp_s", [128, 1], I32)
        if self.debug:
            self.snapb = sb("snapb", [128, 40], F32)
        self.fnc = sb("fnc", [128, 256], BF16)
        self.din = sb("din", [128, DQS], I32)
        self.dsr = sb("dsr", [128, DQS], I32)
        self.dta = sb("dta", [128, DQS], I32)
        self.dsc = sb("dsc", [128, DQS], FP16)
        self.dout = sb("dout", [128, DQS], BF16)
        self.pA = ps("pA", [128, 512], F32)
        self.pB = ps("pB", [128, 512], F32)
        self.pS = [ps("pS0", [128, 512], F32), ps("pS1", [128, 512], F32)]
        self.pO = [ps("pO0", [128, 512], F32), ps("pO1", [128, 512], F32)]
        self.pT = ps("pT", [128, 1024], BF16)
        self.pE = ps("pE", [128, 1024], BF16)
        self.f_pA = ("V", 0)
        self.f_pB = ("V", 0)
        self.f_pS = [("S", 0), ("S", 0)]
        self.f_pO = [("V", 0), ("V", 0)]
        self.f_pT = ("V", 0)
        self.f_pE = ("V", 0)
        self.f_wsl = [("T", 0), ("T", 0)]
        self.m_xb_free = ("V", 0)
        self.m_xn_free = ("T", 0)
        self.m_xnT_store = 0
        self.m_Mb = {}
        self.m_ag = {}
        self.ostores = 0
        self.PT_consumed = [0, 0]
        self.m_tsb_guard = 0
        self.prev_tile_comb = 0
        self.last_exp = 0
        self.m_nmm_prev = 0
        self.m_dq = 0
        self.f_dqbuf = ("V", 0)
        self.m_dqst = 0

    def prologue(self, block):
        qG, sV, sS = self.qG, self.sV, self.sS

        @block.gpsimd
        def _(g):
            qG.inc(g.dma_start(out=self.wqT_b[:], in_=self.wqT[:]))
            qG.inc(g.dma_start(out=self.wkT_b[:], in_=self.wkT[:]))
            qG.inc(g.dma_start(out=self.wvT_b[:], in_=self.wvT[:]))
            qG.inc(g.dma_start(out=self.woT_b[:], in_=self.woT[:]))
            qG.inc(g.dma_start(out=self.idt[:], in_=self.ident[:]))
            qG.inc(g.dma_start(out=self.msk[:], in_=self.mask01[:]))
            qG.inc(g.dma_start(out=self.bd[:], in_=self.bdiag[:]))
            qG.inc(g.dma_start(out=self.bdT[:], in_=self.bdiagT[:]))
            qG.inc(g.dma_start(out=self.ones1[:], in_=self.onesc[:]))
            qG.inc(g.dma_start(out=self.lvs[:], in_=self.lv[:]))
            qG.inc(g.dma_start(out=self.shp_s[:], in_=self.shp[:]))
        self.m_consts = qG.v
        self.m_wcast = 16 * 3

        @block.vector
        def _(v):
            v.memset(self.epsLN[:], 1e-5)
            v.memset(self.eps6[:], 1e-6)
            v.memset(self.lnl[:], LNL)
            v.memset(self.nks[:], 0.0)
            v.wait_ge(qG.h, self.m_consts)
            v.tensor_tensor(self.lsc[0:1, 0:8], self.lvs[0:1, 0:8], self.lvs[0:1, 8:16], Op.mult)
            sV.inc(v.tensor_tensor(self.lsc[0:1, 8:16], self.lvs[0:1, 16:24], self.lvs[0:1, 24:32], Op.mult))
            v.wait_ge(sV.h, sV.v)
            v.reduce_sum(self.sm[0:1, 6:7], self.lsc[0:1, 0:8], AX.X)
            sV.inc(v.reduce_sum(self.sm[0:1, 7:8], self.lsc[0:1, 8:16], AX.X))
        m0 = sV.v

        @block.scalar
        def _(s):
            s.wait_ge(sV.h, m0)
            s.activation(self.lsc[0:1, 0:1], self.sm[0:1, 6:7], AF.Exp)
            sS.inc(s.activation(self.lsc[0:1, 1:2], self.sm[0:1, 7:8], AF.Exp))
        m1 = sS.v

        @block.vector
        def _(v):
            v.wait_ge(sS.h, m1)
            sV.inc(v.tensor_sub(self.lsc[0:1, 2:3], self.lsc[0:1, 0:1], self.lsc[0:1, 1:2]))
            v.wait_ge(sV.h, sV.v)
            sV.inc(v.tensor_scalar_add(self.lsc[0:1, 3:4], self.lsc[0:1, 2:3], L_INIT))
        m2 = sV.v

        @block.gpsimd
        def _(g):
            g.wait_ge(sV.h, m2)
            w = qG.inc(g.dma_start(out=self.lam_d[:], in_=self.lsc[0:1, 3:4]))
            g.wait_ge(qG.h, w)
            self.qMB.inc(g.dma_start(out=self.lamb[:],
                                     in_=self.lam_d.ap().broadcast_to([128, 1])))
        self.m_lamb = self.qMB.v

        jobs = []
        for t in range(2):
            for s0 in range(0, GUF, DQS):
                jobs.append(("gu", t, s0, min(DQS, GUF - s0), 128))
        for t in range(6):
            npart = 128 if t < 5 else 64
            for s0 in range(0, DM, DQS):
                jobs.append(("dn", t, s0, DQS, npart))
        per = (len(jobs) + 7) // 8
        self.dq_per_chunk = [jobs[i * per:(i + 1) * per] for i in range(NCH)]

    def emit_dequant(self, block, kind, t, s0, slen, npart):
        qG, sV = self.qG, self.sV
        pk = self.gup if kind == "gu" else self.dnp
        sc = self.gus if kind == "gu" else self.dns
        cc = self.cc_gu_i if kind == "gu" else self.cc_dn_i
        a = lambda x: x[0:npart, 0:slen]
        st = {}

        @block.gpsimd
        def _(g):
            self.wt(g, self.f_dqbuf)
            qG.inc(g.dma_start(out=self.din[0:npart, 0:slen],
                               in_=pk[t * 128:t * 128 + npart, s0:s0 + slen]))
            st["ld"] = qG.inc(g.dma_start(
                out=self.dsc[0:npart, 0:slen],
                in_=sc[t * 128:t * 128 + npart, s0:s0 + slen]))

        @block.vector
        def _(v):
            v.wait_ge(qG.h, st["ld"])
            if self.m_dqst:
                v.wait_ge(qG.h, self.m_dqst)
            sV.inc(v.tensor_scalar(a(self.dsr), a(self.din), self.shp_s[0:npart, 0:1],
                                   None, Op.logical_shift_right))
            v.wait_ge(sV.h, sV.v)
            v.tensor_scalar(a(self.dta), a(self.dsr), 7, 9, Op.bitwise_and,
                            Op.logical_shift_left)
            sV.inc(v.tensor_scalar(a(self.din), a(self.dsr), 8, 12, Op.bitwise_and,
                                   Op.logical_shift_left))
            v.wait_ge(sV.h, sV.v)
            sV.inc(v.tensor_tensor(a(self.dsr), a(self.dta), a(self.din),
                                   Op.bitwise_or))
            v.wait_ge(sV.h, sV.v)
            fp16v = self.dsr[0:npart, 0:slen].bitcast(FP16)
            sV.inc(v.scalar_tensor_tensor(a(self.dout), fp16v[:, 0::2], 16384.0,
                                          a(self.dsc), Op.mult, Op.mult))
        self.f_dqbuf = ("V", sV.v)
        m_dqv = sV.v

        @block.gpsimd
        def _(g):
            g.wait_ge(sV.h, m_dqv)
            w = 512 if kind == "gu" else 256
            for k in range(slen // w):
                cck = cc[(s0 + k * w) // w]
                self.m_dqst = qG.inc(g.dma_start(
                    out=cck[t * 128:t * 128 + npart, :],
                    in_=self.dout[0:npart, k * w:(k + 1) * w]))
        return self.m_dqst

    def production(self, block, c):
        qS, qG, sT, sV, sS = self.qS, self.qG, self.sT, self.sV, self.sS

        for tt in range(TPC):
            row0 = c * CHUNK + tt * 128

            @block.sync
            def _(e):
                self.wt(e, self.m_xb_free)
                self.m_xld = self.qXT.inc(e.dma_start(
                    out=self.xb[:], in_=self.x_full[row0:row0 + 128, :]))

            @block.vector
            def _(v):
                v.wait_ge(self.qXT.h, self.m_xld)
                for i in range(4):
                    inst = v.bn_stats(self.stats[:, i * 6:(i + 1) * 6],
                                      self.xb[:, i * 512:(i + 1) * 512])
                sV.inc(inst)
                v.wait_ge(sV.h, sV.v)
                sV.inc(v.bn_aggr(self.stats[:, 24:26], self.stats[:, 0:24]))
            m_bn = sV.v

            @block.scalar
            def _(s):
                s.wait_ge(sV.h, m_bn)
                sS.inc(s.activation(self.mrs[:, 1:2], self.stats[:, 25:26], AF.Ln,
                                    bias=self.epsLN[:]))
                s.wait_ge(sS.h, sS.v)
                sS.inc(s.activation(self.mrs[:, 1:2], self.mrs[:, 1:2], AF.Exp,
                                    scale=-0.5))
            m_rstd = sS.v

            @block.vector
            def _(v):
                v.wait_ge(sS.h, m_rstd)
                self.wt(v, self.m_xn_free)
                sV.inc(v.tensor_scalar(self.xn_tok[:], self.xb[:],
                                       self.stats[:, 24:25], self.mrs[:, 1:2],
                                       Op.subtract, Op.mult))
            m_xn = sV.v
            self.m_xb_free = ("V", m_xn)

            for grp in range(2):
                dstp = self.pT if grp == 0 else self.pE
                fmk = self.f_pT if grp == 0 else self.f_pE

                @block.tensor
                def _(t):
                    t.wait_ge(sV.h, m_xn)
                    self.wt(t, fmk)
                    self.fp(t, self.pA[0:1, 0:1])
                    if c == 0 and tt == 0 and grp == 0:
                        t.wait_ge(qG.h, self.m_consts)
                    for j in range(8):
                        f = grp * 8 + j
                        inst = t.transpose(dstp[:, j * 128:(j + 1) * 128],
                                           self.xn_tok[:, f * 128:(f + 1) * 128],
                                           self.idt[:])
                    sT.inc(inst)
                m_tp = sT.v

                @block.vector
                def _(v):
                    v.wait_ge(sT.h, m_tp)
                    self.fv(v)
                    if self.m_xnT_store:
                        v.wait_ge(self.qXS.h, self.m_xnT_store)
                    dst = (self.xnT[:].rearrange("p (f t) -> p f t", t=CHUNK)
                           [:, grp * 8:grp * 8 + 8, tt * 128:(tt + 1) * 128])
                    sV.inc(v.tensor_copy(
                        dst, dstp[:].rearrange("p (f t) -> p f t", t=128)))
                if grp == 0:
                    self.f_pT = ("V", sV.v)
                else:
                    self.f_pE = ("V", sV.v)
            self.m_xn_free = ("T", sT.v)
        m_xnT_done = sV.v

        @block.sync
        def _(e):
            e.wait_ge(sV.h, m_xnT_done)
            for f in range(16):
                self.qXS.inc(e.dma_start(
                    out=self.xnT_all[f * 128:(f + 1) * 128, c * CHUNK:(c + 1) * CHUNK],
                    in_=self.xnT[:, f * CHUNK:(f + 1) * CHUNK]))
        self.m_xnT_store = self.qXS.v

        jobs = ([("q", oc, self.wqT_b) for oc in range(4)]
                + [("k", oc, self.wkT_b) for oc in range(4)]
                + [("vt", oc, self.wvT_b) for oc in range(2)])
        for ji, (kind, oc, wd) in enumerate(jobs):
            sel = ji % 2

            @block.sync
            def _(e):
                self.wt(e, self.f_wsl[sel])
                if c == 0 and ji < 2:
                    e.wait_ge(qG.h, self.m_wcast)
                src = wd.rearrange("(i p) o -> p i o", p=128)[:, :, oc * 128:(oc + 1) * 128]
                self.qSL.inc(e.dma_start(
                    out=self.wsl[sel][:].rearrange("p (i o) -> p i o", o=128), in_=src))
            m_slab = self.qSL.v

            pbank = self.pA if ji % 2 == 0 else self.pB
            fbank = self.f_pA if ji % 2 == 0 else self.f_pB

            @block.tensor
            def _(t):
                t.wait_ge(self.qSL.h, m_slab)
                t.wait_ge(sV.h, m_xnT_done)
                self.wt(t, fbank)
                self.fp(t, pbank[0:1, 0:1])
                for i in range(16):
                    inst = t.matmul(pbank[:], self.wsl[sel][:, i * 128:(i + 1) * 128],
                                    self.xnT[:, i * CHUNK:(i + 1) * CHUNK],
                                    start=(i == 0), stop=(i == 15))
                sT.inc(inst)
            m_mm = sT.v
            self.f_wsl[sel] = ("T", m_mm)

            @block.vector
            def _(v):
                v.wait_ge(sT.h, m_mm)
                self.fv(v)
                if kind == "q":
                    dst = self.qTl[:, oc * CHUNK:(oc + 1) * CHUNK]
                elif kind == "k":
                    dst = self.KT[:, oc * SEQ + c * CHUNK:oc * SEQ + (c + 1) * CHUNK]
                else:
                    dst = self.vts[:, oc * CHUNK:(oc + 1) * CHUNK]
                sV.inc(v.tensor_copy(dst, pbank[:]))
            if ji % 2 == 0:
                self.f_pA = ("V", sV.v)
            else:
                self.f_pB = ("V", sV.v)
            if kind == "q" and oc == 3:
                self.m_qT = sV.v
            if kind == "k" and oc == 3:
                self.m_KT = sV.v
        m_vts = sV.v

        for tt in range(TPC):
            @block.tensor
            def _(t):
                t.wait_ge(sV.h, m_vts)
                self.wt(t, self.f_pT)
                for dvc in range(2):
                    inst = t.transpose(
                        self.pT[:, dvc * 128:(dvc + 1) * 128],
                        self.vts[:, dvc * CHUNK + tt * 128:dvc * CHUNK + tt * 128 + 128],
                        self.idt[:])
                sT.inc(inst)
            m_vtp = sT.v

            @block.vector
            def _(v):
                v.wait_ge(sT.h, m_vtp)
                self.fv(v)
                ti = c * TPC + tt
                sV.inc(v.tensor_copy(self.Vt[:, ti * DH:(ti + 1) * DH], self.pT[:, 0:256]))
            self.f_pT = ("V", sV.v)
        self.m_V = sV.v

        pEf = self.pE[:].bitcast(F32)
        for jn, (kind, fc0) in enumerate([("k", 0), ("k", 2), ("q", 0), ("q", 2)]):
            @block.vector
            def _(v):
                v.wait_ge(sV.h, self.m_KT if kind == "k" else self.m_qT)
                if self.m_nmm_prev:
                    v.wait_ge(sT.h, self.m_nmm_prev)
                for j in range(2):
                    if kind == "k":
                        src = self.KT[:, (fc0 + j) * SEQ + c * CHUNK:
                                      (fc0 + j) * SEQ + (c + 1) * CHUNK]
                    else:
                        src = self.qTl[:, (fc0 + j) * CHUNK:(fc0 + j + 1) * CHUNK]
                    inst = v.tensor_tensor(self.sqn[:, j * 512:(j + 1) * 512],
                                           src, src, Op.mult)
                sV.inc(inst)
            m_sq = sV.v

            @block.tensor
            def _(t):
                t.wait_ge(sV.h, m_sq)
                self.wt(t, self.f_pE)
                t.matmul(pEf[0:1, 0:512], self.ones1[:], self.sqn[:, 0:512],
                         start=True, stop=False)
                sT.inc(t.matmul(pEf[0:1, 0:512], self.ones1[:], self.sqn[:, 512:1024],
                                start=False, stop=True))
            m_nmm = sT.v
            self.m_nmm_prev = m_nmm

            @block.vector
            def _(v):
                v.wait_ge(sT.h, m_nmm)
                self.fv(v)
                if kind == "k":
                    sV.inc(v.reduce_max(self.nks[0:1, 2:3], pEf[0:1, 0:512], AX.X))
                    v.wait_ge(sV.h, sV.v)
                    sV.inc(v.tensor_tensor(self.nks[0:1, jn:jn + 1],
                                           self.nks[0:1, jn:jn + 1],
                                           self.nks[0:1, 2:3], Op.max))
                else:
                    sV.inc(v.tensor_copy(self.nrow[0:1, (jn - 2) * 512:(jn - 1) * 512],
                                         pEf[0:1, 0:512]))
            self.f_pE = ("V", sV.v)
        m_norms = sV.v

        @block.vector
        def _(v):
            v.wait_ge(sV.h, m_norms)
            if c > 0:
                v.wait_ge(qG.h, self.m_Mb[c - 1])
            for tt in range(TPC):
                for mi in range(2):
                    inst = v.reduce_max(self.nks[0:1, 4 + tt * 2 + mi:5 + tt * 2 + mi],
                                        self.nrow[0:1, mi * 512 + tt * 128:
                                                  mi * 512 + (tt + 1) * 128], AX.X)
            v.tensor_copy(self.nks[0:1, 12:20:2],
                          self.nks[0:1, 0:1].broadcast_to([1, 4]))
            sV.inc(v.tensor_copy(self.nks[0:1, 13:21:2],
                                 self.nks[0:1, 1:2].broadcast_to([1, 4])))
            v.wait_ge(sV.h, sV.v)
            sV.inc(v.tensor_tensor(self.Mb1[0:1, :], self.nks[0:1, 4:12],
                                   self.nks[0:1, 12:20], Op.mult))
        m_mprod = sV.v

        @block.scalar
        def _(s):
            s.wait_ge(sV.h, m_mprod)
            sS.inc(s.activation(self.Mb1[0:1, :], self.Mb1[0:1, :], AF.Ln))
            s.wait_ge(sS.h, sS.v)
            sS.inc(s.activation(self.Mb1[0:1, :], self.Mb1[0:1, :], AF.Exp, scale=0.5))
        m_mexp = sS.v

        @block.vector
        def _(v):
            v.wait_ge(sS.h, m_mexp)
            sV.inc(v.tensor_scalar_mul(self.Mb1[0:1, :], self.Mb1[0:1, :], -0.0625))
        m_mb = sV.v

        @block.gpsimd
        def _(g):
            g.wait_ge(sV.h, m_mb)
            if self.last_exp:
                g.wait_ge(sS.h, self.last_exp)
            w = qG.inc(g.dma_start(out=self.mb_d[:], in_=self.Mb1[0:1, :]))
            g.wait_ge(qG.h, w)
            self.m_Mb[c] = self.qMB.inc(g.dma_start(
                out=self.Mbb[:], in_=self.mb_d.ap().broadcast_to([128, 8])))
        for job in self.dq_per_chunk[c]:
            self.m_dq = self.emit_dequant(block, *job)
        if c == 8:
            @block.gpsimd
            def _(g):
                g.wait_ge(qG.h, self.m_dq)
                for c2 in range(22):
                    g.collective_compute(
                        "AllGather", Op.bypass, replica_groups=[list(range(8))],
                        ins=[self.cc_gu_i[c2].ap().opt()],
                        outs=[self.cc_gu_o[c2].ap().opt()],
                    ).then_inc(self.sC.h, 1)
                    self.sC.v += 1
                self.m_ag["gu"] = self.sC.v
                for c2 in range(8):
                    g.collective_compute(
                        "AllGather", Op.bypass, replica_groups=[list(range(8))],
                        ins=[self.cc_dn_i[c2].ap().opt()],
                        outs=[self.cc_dn_o[c2].ap().opt()],
                    ).then_inc(self.sC.h, 1)
                    self.sC.v += 1
                self.m_ag["dn"] = self.sC.v

    def attn_tile(self, block, c, tt):
        qS, qG, sT, sV, sS = self.qS, self.qG, self.sT, self.sV, self.sS
        qt = c * TPC + tt
        kend = (qt + 1) * 128
        nfull = (kend - 128) // 512
        rem = (kend - 128) % 512
        batches = [(i * 512, 512, False) for i in range(nfull)]
        batches.append((nfull * 512, rem + 128, True))
        nslot = nfull + (1 if rem else 0)

        for mi in range(2):
            E = self.E1 if mi == 0 else self.E2
            fc0 = 0 if mi == 0 else 2
            rc = self.racc[mi]
            for bi, (k0, klen, last) in enumerate(batches):
                sel = bi % 2
                pb = self.pS[sel]

                @block.tensor
                def _(t):
                    if mi == 0 and bi == 0:
                        t.wait_ge(sV.h, self.m_V)
                    self.wt(t, self.f_pS[sel])
                    self.fp(t, pb[0:1, 0:1])
                    t.matmul(pb[:, 0:klen],
                             self.qTl[:, fc0 * CHUNK + tt * 128:fc0 * CHUNK + (tt + 1) * 128],
                             self.KT[:, fc0 * SEQ + k0:fc0 * SEQ + k0 + klen],
                             start=True, stop=False)
                    sT.inc(t.matmul(
                        pb[:, 0:klen],
                        self.qTl[:, (fc0 + 1) * CHUNK + tt * 128:(fc0 + 1) * CHUNK + (tt + 1) * 128],
                        self.KT[:, (fc0 + 1) * SEQ + k0:(fc0 + 1) * SEQ + k0 + klen],
                        start=False, stop=True))
                m_smm = sT.v

                @block.scalar
                def _(s):
                    s.wait_ge(sT.h, m_smm)
                    self.fs(s)
                    if mi == 0 and bi == 0:
                        s.wait_ge(self.qMB.h, self.m_Mb[c])
                        if self.prev_tile_comb:
                            s.wait_ge(sV.h, self.prev_tile_comb)
                        if self.debug and getattr(self, "dbg_store", 0):
                            s.wait_ge(qS.h, self.dbg_store)
                    bias = self.Mbb[:, tt * 2 + mi:tt * 2 + mi + 1]
                    nd = klen - 128 if last else klen
                    inst = None
                    if nd > 0:
                        inst = s.activation(E[:, k0:k0 + nd], pb[:, 0:nd], AF.Exp,
                                            bias=bias, scale=0.0625,
                                            accum_out=rc[:, bi:bi + 1])
                    if last:
                        inst = s.activation(E[:, k0 + nd:k0 + nd + 128],
                                            pb[:, nd:nd + 128], AF.Exp,
                                            bias=bias, scale=0.0625)
                    sS.inc(inst)
                self.f_pS[sel] = ("S", sS.v)
        m_E = sS.v
        self.last_exp = m_E

        @block.vector
        def _(v):
            v.wait_ge(sS.h, m_E)
            v.wait_ge(self.qMB.h, self.m_lamb)
            if self.debug and qt == self.dbg_tile:
                v.tensor_copy(self.snapb[:, 0:18], self.racc[0][:])
                v.tensor_copy(self.snapb[:, 18:36], self.racc[1][:])
                v.tensor_copy(self.snapb[:, 36:37], self.lamb[:])
            for mi in range(2):
                E = self.E1 if mi == 0 else self.E2
                rc = self.racc[mi]
                sV.inc(v.tensor_tensor(E[:, kend - 128:kend], E[:, kend - 128:kend],
                                       self.msk[:], Op.mult))
                v.wait_ge(sV.h, sV.v)
                sV.inc(v.reduce_sum(self.sm[:, mi:mi + 1], E[:, kend - 128:kend], AX.X))
                v.wait_ge(sV.h, sV.v)
                if nslot > 0:
                    sV.inc(v.reduce_sum(self.sm[:, 2 + mi:3 + mi], rc[:, 0:nslot], AX.X))
                    v.wait_ge(sV.h, sV.v)
                    sV.inc(v.tensor_tensor(self.sm[:, 2 + mi:3 + mi],
                                           self.sm[:, 2 + mi:3 + mi],
                                           self.sm[:, mi:mi + 1], Op.add))
                    v.wait_ge(sV.h, sV.v)
                else:
                    sV.inc(v.tensor_copy(self.sm[:, 2 + mi:3 + mi],
                                         self.sm[:, mi:mi + 1]))
                    v.wait_ge(sV.h, sV.v)
            sV.inc(v.reciprocal(self.sm[:, 4:5], self.sm[:, 2:3]))
            v.wait_ge(sV.h, sV.v)
            sV.inc(v.reciprocal(self.sm[:, 5:6], self.sm[:, 3:4]))
            v.wait_ge(sV.h, sV.v)
            sV.inc(v.tensor_tensor(self.sm[:, 5:6], self.sm[:, 5:6], self.lamb[:],
                                   Op.mult))
        m_ab = sV.v

        osel = qt % 2
        po = self.pO[osel]
        nsl = (kend + 511) // 512
        for si in range(nsl):
            s0 = si * 512
            sl = min(512, kend - s0)
            psel = si % 2

            @block.vector
            def _(v):
                if si == 0:
                    v.wait_ge(sV.h, m_ab)
                if self.m_tsb_guard:
                    v.wait_ge(sT.h, self.m_tsb_guard)
                v.tensor_scalar(self.tsb1[:, 0:sl], self.E1[:, s0:s0 + sl],
                                self.sm[:, 4:5], None, Op.mult)
                sV.inc(v.tensor_scalar(self.tsb2[:, 0:sl], self.E2[:, s0:s0 + sl],
                                       self.sm[:, 5:6], None, Op.mult))
                v.wait_ge(sV.h, sV.v)
                sV.inc(v.tensor_sub(self.tsbP[:, 0:sl], self.tsb1[:, 0:sl],
                                    self.tsb2[:, 0:sl]))
            m_P = sV.v

            dstp = self.pT if psel == 0 else self.pE
            fmk = self.f_pT if psel == 0 else self.f_pE

            @block.tensor
            def _(t):
                t.wait_ge(sV.h, m_P)
                self.wt(t, fmk)
                self.fp(t, self.pS[0][0:1, 0:1])
                for j in range(sl // 128):
                    inst = t.transpose(dstp[:, j * 128:(j + 1) * 128],
                                       self.tsbP[:, j * 128:(j + 1) * 128], self.idt[:])
                sT.inc(inst)
            m_ptt = sT.v
            self.m_tsb_guard = m_ptt

            @block.vector
            def _(v):
                v.wait_ge(sT.h, m_ptt)
                self.fv(v)
                if self.PT_consumed[psel]:
                    v.wait_ge(sT.h, self.PT_consumed[psel])
                sV.inc(v.tensor_copy(self.PT[psel][:, 0:sl], dstp[:, 0:sl]))
            m_ptev = sV.v
            if psel == 0:
                self.f_pT = ("V", m_ptev)
            else:
                self.f_pE = ("V", m_ptev)

            @block.tensor
            def _(t):
                t.wait_ge(sV.h, m_ptev)
                if si == 0:
                    self.wt(t, self.f_pO[osel])
                self.fp(t, self.pS[1][0:1, 0:1])
                for j in range(sl // 128):
                    kt = s0 // 128 + j
                    for dvc in range(2):
                        inst = t.matmul(
                            po[:, dvc * 128:(dvc + 1) * 128],
                            self.Vt[:, kt * DH + dvc * 128:kt * DH + (dvc + 1) * 128],
                            self.PT[psel][:, j * 128:(j + 1) * 128],
                            start=(kt == 0 and dvc == 0),
                            stop=(kt == kend // 128 - 1 and dvc == 1),
                            skip_group_check=True)
                sT.inc(inst)
            self.PT_consumed[psel] = sT.v
        m_oacc = sT.v

        @block.vector
        def _(v):
            v.wait_ge(sT.h, m_oacc)
            self.fv(v)
            sV.inc(v.tensor_copy(self.o_s[:], po[:, 0:256]))
            v.wait_ge(sV.h, sV.v)
            sV.inc(v.tensor_tensor(self.sqo[:], self.o_s[:], self.o_s[:], Op.mult))
        m_sqo = sV.v
        self.f_pO[osel] = ("V", m_sqo)

        pEf = self.pE[:].bitcast(F32)

        @block.tensor
        def _(t):
            t.wait_ge(sV.h, m_sqo)
            self.wt(t, self.f_pE)
            for dvc in range(2):
                inst = t.matmul(pEf[0:4, dvc * 128:(dvc + 1) * 128], self.bd[:],
                                self.sqo[:, dvc * 128:(dvc + 1) * 128],
                                start=True, stop=True)
            sT.inc(inst)
        m_gs = sT.v

        @block.scalar
        def _(s):
            s.wait_ge(sT.h, m_gs)
            self.fs(s)
            sS.inc(s.activation(self.rbs[0:4, 0:256], pEf[0:4, 0:256], AF.Ln,
                                scale=1.0 / 32.0, bias=self.eps6[0:4, 0:1]))
            s.wait_ge(sS.h, sS.v)
            sS.inc(s.activation(self.rbs[0:4, 0:256], self.rbs[0:4, 0:256], AF.Exp,
                                scale=-0.5, bias=self.lnl[0:4, 0:1]))
        m_rstd = sS.v

        @block.tensor
        def _(t):
            t.wait_ge(sS.h, m_rstd)
            for dvc in range(2):
                inst = t.matmul(pEf[:, 256 + dvc * 128:256 + (dvc + 1) * 128],
                                self.bdT[:], self.rbs[0:4, dvc * 128:(dvc + 1) * 128],
                                start=True, stop=True)
            sT.inc(inst)
        m_bcast = sT.v

        @block.vector
        def _(v):
            v.wait_ge(sT.h, m_bcast)
            self.fv(v)
            if self.ostores:
                v.wait_ge(self.qOS.h, self.ostores)
            sV.inc(v.tensor_copy(self.rbs[:, 0:256], pEf[:, 256:512]))
            v.wait_ge(sV.h, sV.v)
            sV.inc(v.tensor_tensor(self.o_n[:], self.o_s[:], self.rbs[:, 0:256],
                                   Op.mult))
        m_on = sV.v
        self.f_pE = ("V", m_on)

        @block.gpsimd
        def _(g):
            g.wait_ge(sV.h, m_on)
            jb = qt // 8
            jc = qt % 8
            for dvc in range(2):
                self.qOS.inc(g.dma_start(
                    out=self.cc_o_i[jc][256 * jb + dvc * 128:
                                        256 * jb + (dvc + 1) * 128, :],
                    in_=self.o_n[:, dvc * 128:(dvc + 1) * 128]))
        self.ostores = self.qOS.v

        if self.debug and qt == self.dbg_tile:
            @block.sync
            def _(e):
                e.wait_ge(sV.h, m_on)
                qS.inc(e.dma_start(out=self.dbg_e1[:], in_=self.E1[:]))
                qS.inc(e.dma_start(out=self.dbg_e2[:], in_=self.E2[:]))
                qS.inc(e.dma_start(out=self.dbg_sm[:], in_=self.sm[:]))
                qS.inc(e.dma_start(out=self.dbg_on[:], in_=self.o_n[:]))
                qS.inc(e.dma_start(out=self.dbg_xnt[:], in_=self.xnT[:]))
                qS.inc(e.dma_start(out=self.dbg_rc1[:], in_=self.racc[0][:]))
                qS.inc(e.dma_start(out=self.dbg_rc2[:], in_=self.racc[1][:]))
                qS.inc(e.dma_start(out=self.dbg_mbb[:], in_=self.Mbb[:]))
                qS.inc(e.dma_start(out=self.dbg_snap[:], in_=self.snapb[:]))
                qS.inc(e.dma_start(out=self.dbg_os[:], in_=self.o_s[:]))
                qS.inc(e.dma_start(out=self.dbg_rbs[:], in_=self.rbs[:]))
            self.dbg_store = qS.v

        self.prev_tile_comb = sV.v

    def finish1(self, block):
        qS, qG = self.qS, self.qG

        if self.debug:
            @block.sync
            def _(e):
                e.wait_ge(self.sV.h, self.m_V)
                qS.inc(e.dma_start(out=self.dbg_kt[:], in_=self.KT[:]))
                qS.inc(e.dma_start(out=self.dbg_v[:], in_=self.Vt[:]))

        @block.gpsimd
        def _(g):
            g.wait_ge(self.qOS.h, self.ostores)
            from concourse.aot_env import allow_ici_all_to_all
            with allow_ici_all_to_all():
                for c2 in range(8):
                    g.collective_compute(
                        "AllToAll", Op.bypass, replica_groups=[list(range(8))],
                        ins=[self.cc_o_i[c2].ap().opt()],
                        outs=[self.cc_o_o[c2].ap().opt()],
                    ).then_inc(self.sC.h, 1)
                    self.sC.v += 1
            self.m_ag["o"] = self.sC.v

    def block2a(self, C):
        nc = self.nc
        qS, qG, sT, sV, sS = self.qS, self.qG, self.sT, self.sV, self.sS
        sn = self.snap1
        sb = lambda n, s, t: C.enter_context(nc.sbuf_tensor(n, s, t))
        ps = lambda n, s, t: C.enter_context(nc.psum_tensor(n, s, t))
        xnF = sb("xnF", [128, 16 * TSL], BF16)
        wgG = [sb("wgG0", [128, 16 * 512], BF16), sb("wgG1", [128, 16 * 512], BF16)]
        wgU = [sb("wgU0", [128, 16 * 512], BF16), sb("wgU1", [128, 16 * 512], BF16)]
        gat = sb("gat", [128, TSL], BF16)
        self.fnc = sb("fnc2a", [128, 256], BF16)
        htmp = sb("htmp", [128, TSL], BF16)
        cos = sb("cos", [1, 1], I32)
        pG = [ps("pG0", [128, 1024], F32), ps("pG1", [128, 1024], F32)]
        pU = [ps("pU0", [128, 1024], F32), ps("pU1", [128, 1024], F32)]
        block = C.enter_context(nc.Block())

        @block.gpsimd
        def _(g):
            self.wait_snap(g, sn)
            qG.inc(g.dma_start(out=cos[:], in_=self.co[:]))
        m_co = qG.v

        @block.gpsimd
        def _(g):
            g.wait_ge(qG.h, m_co)
            with g.register("r0") as r0, g.register("rt0") as rt0:
                g.reg_load(r0, cos[0:1, 0:1])
                for i in range(16):
                    g.reg_mov(rt0, i * SEQ * 128)
                    g.reg_add(rt0, rt0, r0)
                    qG.inc(g.dma_start(
                        out=xnF[:, i * TSL:(i + 1) * TSL],
                        in_=bass.AP(self.xnT_all, rt0, [[SEQ, 128], [1, TSL]])))
        m_xnF = qG.v

        def load_slab(e, dstbuf, g0):
            src = self.cc_gu_o[g0].rearrange("(i p) o -> p i o", p=128)
            return qS.inc(e.dma_start(
                out=dstbuf[:].rearrange("p (i o) -> p i o", o=512), in_=src))

        f_wg = [("T", 0), ("T", 0)]
        m_wg = {}
        m_wu = {}
        f_pG = [("S", 0), ("S", 0)]
        f_pU = [("V", 0), ("V", 0)]
        hstores = 0
        m_gmm = 0
        for i in range(44):
            gsl = i // 4
            usl = 11 + i // 4
            gsel = (i // 4) % 2
            if i % 4 == 0:
                @block.sync
                def _(e):
                    if i == 0:
                        self.wait_snap(e, sn)
                        e.wait_ge(self.sC.h, self.m_ag["gu"])
                    self.wt(e, f_wg[gsel])
                    m_wg[gsl] = load_slab(e, wgG[gsel], gsl)
                    m_wu[usl] = load_slab(e, wgU[gsel], usl)
            psel = i % 2

            @block.tensor
            def _(t):
                t.wait_ge(qS.h, m_wu[usl])
                if i == 0:
                    self.wait_snap(t, sn)
                    t.wait_ge(qG.h, m_xnF)
                off = (i % 4) * 128
                self.wt(t, f_pG[psel])
                for ii in range(16):
                    for half in range(2):
                        inst = t.matmul(pG[psel][:, half * 512:(half + 1) * 512],
                                        wgG[gsel][:, ii * 512 + off:ii * 512 + off + 128],
                                        xnF[:, ii * TSL + half * 512:ii * TSL + (half + 1) * 512],
                                        start=(ii == 0), stop=(ii == 15))
                sT.inc(inst)
                nonlocal m_gmm
                m_gmm = sT.v
                self.wt(t, f_pU[psel])
                for ii in range(16):
                    for half in range(2):
                        inst = t.matmul(pU[psel][:, half * 512:(half + 1) * 512],
                                        wgU[gsel][:, ii * 512 + off:ii * 512 + off + 128],
                                        xnF[:, ii * TSL + half * 512:ii * TSL + (half + 1) * 512],
                                        start=(ii == 0), stop=(ii == 15))
                sT.inc(inst)
            m_umm = sT.v
            if i % 4 == 3:
                f_wg[gsel] = ("T", m_umm)

            @block.scalar
            def _(s):
                s.wait_ge(sT.h, m_gmm)
                self.fs(s)
                sS.inc(s.activation(gat[:], pG[psel][:], AF.Silu))
            m_silu = sS.v
            f_pG[psel] = ("S", m_silu)

            @block.vector
            def _(v):
                v.wait_ge(sS.h, m_silu)
                v.wait_ge(sT.h, m_umm)
                self.fv(v)
                if i == 0:
                    self.wait_snap(v, sn)
                if hstores:
                    v.wait_ge(qG.h, hstores)
                sV.inc(v.tensor_tensor(htmp[:], gat[:], pU[psel][:], Op.mult))
            m_h = sV.v
            f_pU[psel] = ("V", m_h)

            @block.gpsimd
            def _(g):
                g.wait_ge(sV.h, m_h)
                qG.inc(g.dma_start(out=self.h_scr[i * 128:(i + 1) * 128, :], in_=htmp[:]))
            hstores = qG.v
        self.m_hdone = hstores

    def block2b(self, C):
        nc = self.nc
        qS, qG, sT, sV, sS = self.qS, self.qG, self.sT, self.sV, self.sS
        sn = self.snap2a
        sb = lambda n, s, t: C.enter_context(nc.sbuf_tensor(n, s, t))
        ps = lambda n, s, t: C.enter_context(nc.psum_tensor(n, s, t))
        hQ = sb("hQ", [128, 44 * 512], BF16)
        wdn = [sb("wdn0", [128, 1024], BF16), sb("wdn1", [128, 1024], BF16)]
        yq = sb("yq", [128, 4 * 2048], F32)
        self.g_all = sb("g_all", [128, 8 * 2048], BF16)
        sqs = sb("sqs", [128, 2048], BF16)
        self.fnc = sb("fnc2b", [128, 256], BF16)
        rsd = sb("rsd", [128, 2], F32)
        ysb = sb("ysb", [128, 2048], BF16)
        eps6b = sb("eps6b", [128, 1], F32)
        if self.debug:
            self.dbg_py_s = sb("dbg_py_s", [128, 1024], F32)
        pY = [ps("pY%d" % i, [128, 1024], F32) for i in range(4)]
        block = C.enter_context(nc.Block())

        @block.vector
        def _(v):
            self.wait_snap(v, sn)
            v.memset(eps6b[:], 1e-6)
            sV.inc(v.engine_nop())
        m_b2b = sV.v

        f_pY = [("V", 0)] * 4
        first_t = [True]
        for quad in range(2):
            @block.gpsimd
            def _(g):
                if quad == 0:
                    self.wait_snap(g, sn)
                    g.wait_ge(self.sC.h, self.m_ag["dn"])
                g.wait_ge(qG.h, self.m_hdone)
                if quad == 1:
                    g.wait_ge(sV.h, self.m_yev)
                for i in range(44):
                    self.qH.inc(g.dma_start(
                        out=hQ[:, i * 512:(i + 1) * 512],
                        in_=self.h_scr[i * 128:(i + 1) * 128,
                                       quad * 512:(quad + 1) * 512]))
            m_hq = self.qH.v

            for half in range(2):
                m_wd = {}
                for i in range(44):
                    wsel = i % 2

                    @block.gpsimd
                    def _(g):
                        if i >= 2:
                            g.wait_ge(sT.h, m_wd[i - 2][1])
                        for k in range(4):
                            mw = self.qW.inc(g.dma_start(
                                out=wdn[wsel][:, k * 256:(k + 1) * 256],
                                in_=self.cc_dn_o[half * 4 + k]
                                [i * 128:(i + 1) * 128, :]))
                        m_wd[i] = [mw, 0]

                    @block.tensor
                    def _(t):
                        t.wait_ge(self.qW.h, m_wd[i][0])
                        if first_t[0]:
                            self.wait_snap(t, sn)
                            first_t[0] = False
                        if i == 0:
                            t.wait_ge(self.qH.h, m_hq)
                        for t4 in range(4):
                            pyi = t4
                            if i == 0:
                                self.wt(t, f_pY[pyi])
                            for nn in range(2):
                                inst = t.matmul(
                                    pY[pyi][:, nn * 512:(nn + 1) * 512],
                                    hQ[:, i * 512 + t4 * 128:i * 512 + (t4 + 1) * 128],
                                    wdn[wsel][:, nn * 512:(nn + 1) * 512],
                                    start=(i == 0), stop=(i == 43))
                        sT.inc(inst)
                    m_wd[i][1] = sT.v
                m_ymm = sT.v

                if self.debug and quad == 0 and half == 0:
                    @block.vector
                    def _(v):
                        v.wait_ge(sT.h, m_ymm)
                        self.fv(v)
                        sV.inc(v.tensor_copy(self.dbg_py_s[:], pY[3][:]))
                    m_pydmp = sV.v

                    @block.sync
                    def _(e):
                        e.wait_ge(sV.h, m_pydmp)
                        qS.inc(e.dma_start(out=self.dbg_y3[:], in_=self.dbg_py_s[:]))

                @block.vector
                def _(v):
                    v.wait_ge(sT.h, m_ymm)
                    self.fv(v, big=True)
                    for t4 in range(4):
                        for nn in range(2):
                            inst = v.tensor_copy(
                                yq[:, t4 * 2048 + half * 1024 + nn * 512:
                                   t4 * 2048 + half * 1024 + (nn + 1) * 512],
                                pY[t4][:, nn * 512:(nn + 1) * 512])
                        sV.inc(inst)
                    for t4 in range(4):
                        f_pY[t4] = ("V", sV.v)
            m_yev = sV.v
            self.m_yev = m_yev

            if self.debug and quad == 0:
                @block.sync
                def _(e):
                    e.wait_ge(sV.h, m_yev)
                    qS.inc(e.dma_start(out=self.dbg_y2[:],
                                       in_=yq[:, 3 * 2048:3 * 2048 + 1024]))
                m_dy2 = qS.v

                @block.vector
                def _(v):
                    v.wait_ge(qS.h, m_dy2)
                    sV.inc(v.engine_nop())

            for t4 in range(4):
                tile = quad * 4 + t4

                @block.vector
                def _(v):
                    v.wait_ge(sV.h, m_yev)
                    ysl = yq[:, t4 * 2048:(t4 + 1) * 2048]
                    sV.inc(v.tensor_tensor(sqs[:], ysl, ysl, Op.mult))
                    v.wait_ge(sV.h, sV.v)
                    sV.inc(v.reduce_sum(rsd[:, 0:1], sqs[:], AX.X))
                m_ss = sV.v

                @block.scalar
                def _(s):
                    s.wait_ge(sV.h, m_ss)
                    if tile == 0:
                        self.wait_snap(s, sn)
                    sS.inc(s.activation(rsd[:, 1:2], rsd[:, 0:1], AF.Ln,
                                        scale=1.0 / 2048.0, bias=eps6b[:]))
                    s.wait_ge(sS.h, sS.v)
                    sS.inc(s.activation(rsd[:, 1:2], rsd[:, 1:2], AF.Exp, scale=-0.5))
                m_rst = sS.v

                @block.vector
                def _(v):
                    v.wait_ge(sS.h, m_rst)
                    sV.inc(v.tensor_scalar_mul(ysb[:], yq[:, t4 * 2048:(t4 + 1) * 2048],
                                               rsd[:, 1:2]))
                m_ys = sV.v

                @block.scalar
                def _(s):
                    s.wait_ge(sV.h, m_ys)
                    sS.inc(s.activation(self.g_all[:, tile * 2048:(tile + 1) * 2048],
                                        ysb[:], AF.Silu))
        self.m_g = sS.v
        if self.debug:
            @block.sync
            def _(e):
                e.wait_ge(sS.h, self.m_g)
                qS.inc(e.dma_start(out=self.dbg_g[:], in_=self.g_all[:, 3 * 2048:4 * 2048]))
                qS.inc(e.dma_start(out=self.dbg_y[:], in_=yq[:, 0:2048]))
                qS.inc(e.dma_start(out=self.dbg_wgu[:], in_=hQ[:, 0:1024]))
                qS.inc(e.dma_start(out=self.dbg_wdn[:], in_=self.cc_dn_o[0][:, :]))

    def block2c(self, C):
        nc = self.nc
        qS, qG, sT, sV, sS = self.qS, self.qG, self.sT, self.sV, self.sS
        sn = self.snap2b
        sb = lambda n, s, t: C.enter_context(nc.sbuf_tensor(n, s, t))
        ps = lambda n, s, t: C.enter_context(nc.psum_tensor(n, s, t))
        oT = sb("oT", [128, 16 * TSL], BF16)
        wos = [sb("wos0", [128, 1024], BF16), sb("wos1", [128, 1024], BF16)]
        xre = sb("xre", [128, 1024], F32)
        tmu = sb("tmu", [128, 1024], F32)
        self.fnc = sb("fnc2c", [128, 256], BF16)
        outs = sb("outs", [128, 1024], F32)
        cos2 = sb("cos2", [1, 1], I32)
        pSm = [ps("pSm%d" % i, [128, 1024], F32) for i in range(4)]
        block = C.enter_context(nc.Block())

        @block.gpsimd
        def _(g):
            self.wait_snap(g, sn)
            qG.inc(g.dma_start(out=cos2[:], in_=self.co[:]))
        m_co = qG.v

        @block.gpsimd
        def _(g):
            self.wait_snap(g, sn)
            g.wait_ge(self.sC.h, self.m_ag["o"])
            for i in range(16):
                for c2 in range(8):
                    self.qOT.inc(g.dma_start(
                        out=oT[:, i * TSL + c2 * 128:i * TSL + (c2 + 1) * 128],
                        in_=self.cc_o_o[c2][i * 128:(i + 1) * 128, :]))
        m_oT = self.qOT.v

        outst = 0
        m_xre = 0
        f_pS = [("V", 0)] * 4
        first_t = [True]
        first_v = [True]
        for quad in range(2):
            for half in range(2):
                m_wo = {}
                for i in range(16):
                    wsel = i % 2

                    @block.gpsimd
                    def _(g):
                        if i >= 2:
                            g.wait_ge(sT.h, m_wo[i - 2][1])
                        m_wo[i] = [self.qWo.inc(g.dma_start(
                            out=wos[wsel][:],
                            in_=self.woT_b[i * 128:(i + 1) * 128,
                                           half * 1024:(half + 1) * 1024])), 0]

                    @block.tensor
                    def _(t):
                        t.wait_ge(self.qWo.h, m_wo[i][0])
                        if first_t[0]:
                            self.wait_snap(t, sn)
                            t.wait_ge(self.qOT.h, m_oT)
                            first_t[0] = False
                        for t4 in range(4):
                            pyi = t4
                            if i == 0:
                                self.wt(t, f_pS[pyi])
                            for nn in range(2):
                                inst = t.matmul(
                                    pSm[pyi][:, nn * 512:(nn + 1) * 512],
                                    oT[:, i * TSL + quad * 512 + t4 * 128:
                                       i * TSL + quad * 512 + (t4 + 1) * 128],
                                    wos[wsel][:, nn * 512:(nn + 1) * 512],
                                    start=(i == 0), stop=(i == 15))
                        sT.inc(inst)
                    m_wo[i][1] = sT.v
                m_smm = sT.v

                if self.debug and quad == 0 and half == 0:
                    @block.vector
                    def _(v):
                        v.wait_ge(sT.h, m_smm)
                        self.fv(v)
                        sV.inc(v.tensor_copy(tmu[:], pSm[3][:]))
                    m_sdmp = sV.v

                    @block.sync
                    def _(e):
                        e.wait_ge(sV.h, m_sdmp)
                        qS.inc(e.dma_start(out=self.dbg_s[:], in_=tmu[:]))
                        qS.inc(e.dma_start(out=self.dbg_ot[:], in_=oT[:]))
                    m_sdst = qS.v

                    @block.vector
                    def _(v):
                        v.wait_ge(qS.h, m_sdst)
                        sV.inc(v.engine_nop())

                for t4 in range(4):
                    tile = quad * 4 + t4

                    @block.gpsimd
                    def _(g):
                        if m_xre and outst:
                            g.wait_ge(sV.h, self.m_out_last)
                        self.qX.inc(g.dma_start(
                            out=xre[:],
                            in_=self.x_sl[tile * 128:(tile + 1) * 128,
                                          half * 1024:(half + 1) * 1024]))
                    m_xre = self.qX.v

                    @block.vector
                    def _(v):
                        v.wait_ge(sT.h, m_smm)
                        self.fv(v, big=True)
                        v.wait_ge(self.qX.h, m_xre)
                        if first_v[0]:
                            self.wait_snap(v, sn)
                            v.wait_ge(sS.h, self.m_g)
                            first_v[0] = False
                        if outst:
                            v.wait_ge(self.qU.h, outst)
                        pyi = t4
                        sV.inc(v.tensor_tensor(tmu[:], pSm[pyi][:],
                                               self.g_all[:, tile * 2048 + half * 1024:
                                                          tile * 2048 + (half + 1) * 1024],
                                               Op.mult))
                        v.wait_ge(sV.h, sV.v)
                        sV.inc(v.tensor_tensor(outs[:], tmu[:], xre[:], Op.add))
                    m_out = sV.v
                    self.m_out_last = m_out
                    f_pS[t4] = ("V", m_out)

                    @block.gpsimd
                    def _(g):
                        g.wait_ge(sV.h, m_out)
                        self.qU.inc(g.dma_start(
                            out=self.out_c[tile * 128:(tile + 1) * 128,
                                           half * 1024:(half + 1) * 1024],
                            in_=outs[:]))
                    outst = self.qU.v

        @block.gpsimd
        def _(g):
            for _, c in self._all_sems():
                if c.v > 0:
                    g.wait_ge(c.h, c.v)

    def build(self):
        self.declare()
        with ExitStack() as OC:
            self.sems(OC)
            with ExitStack() as C1:
                self.alloc1(C1)
                block = C1.enter_context(self.nc.Block())
                self.prologue(block)
                for c in range(NCH):
                    self.production(block, c)
                    for tt in range(TPC):
                        self.attn_tile(block, c, tt)
                self.finish1(block)
                self.snap1 = self.snap()
            with ExitStack() as C2a:
                self.block2a(C2a)
                self.snap2a = self.snap()
            with ExitStack() as C2b:
                self.block2b(C2b)
                self.snap2b = self.snap()
            with ExitStack() as C2c:
                self.block2c(C2c)
        return self.nc



def _prep_core(pre, h):
    x, wq, wk, wv, wo = pre["x"], pre["wq"], pre["wk"], pre["wv"], pre["wo"]
    cols = np.r_[h * 256:(h + 1) * 256, 2048 + h * 256:2048 + (h + 1) * 256]
    d = {
        "x_full": x,
        "x_sl": np.ascontiguousarray(x[h * TSL:(h + 1) * TSL]),
        "wqT": np.ascontiguousarray(wq.T[:, cols]),
        "wkT": np.ascontiguousarray(wk.T[:, cols]),
        "wvT": np.ascontiguousarray(wv.T[:, h * 256:(h + 1) * 256]),
        "woT": pre["woT"],
        "lv": pre["lv"],
        "ident": np.eye(128, dtype=np.float32).astype(BF),
        "mask01": np.tril(np.ones((128, 128), np.float32)).astype(BF),
        "bdiag": (np.arange(128)[:, None] // 32 == np.arange(4)[None, :])
                 .astype(np.float32),
        "bdiagT": np.ascontiguousarray(
            (np.arange(128)[:, None] // 32 == np.arange(4)[None, :])
            .astype(np.float32).T),
        "onesc": np.ones((128, 1), np.float32).astype(BF),
        "shp": ((np.arange(128) % 8) * 4).astype(np.int32).reshape(128, 1),
        "co": np.array([[h * TSL]], dtype=np.int32),
        "gup": np.ascontiguousarray(pre["gupR"][h * 256:(h + 1) * 256]),
        "gus": np.ascontiguousarray(pre["gusR"][h * 256:(h + 1) * 256]),
        "dnp": np.ascontiguousarray(pre["dnpR"][h * 704:(h + 1) * 704]),
        "dns": np.ascontiguousarray(pre["dnsR"][h * 704:(h + 1) * 704]),
    }
    return d


def prep_all(inputs):
    x = np.ascontiguousarray(np.asarray(inputs["L0_attn_x"], dtype=np.float32)
                             .reshape(SEQ, DM))
    pre = {
        "x": x,
        "wq": np.asarray(inputs["L0_attn_wq"], dtype=np.float32),
        "wk": np.asarray(inputs["L0_attn_wk"], dtype=np.float32),
        "wv": np.asarray(inputs["L0_attn_wv"], dtype=np.float32),
        "wo": np.asarray(inputs["L0_attn_wo"], dtype=np.float32),
        "lv": np.concatenate([np.asarray(inputs["L0_attn_l_q1"], dtype=np.float32),
                              np.asarray(inputs["L0_attn_l_k1"], dtype=np.float32),
                              np.asarray(inputs["L0_attn_l_q2"], dtype=np.float32),
                              np.asarray(inputs["L0_attn_l_k2"], dtype=np.float32)]).reshape(1, 32),
    }
    pre["woT"] = np.ascontiguousarray(pre["wo"].T)
    pre["gupR"] = np.repeat(np.asarray(inputs["L0_ffn_gate_up_packed"]).T, 8, axis=0)
    pre["dnpR"] = np.repeat(np.asarray(inputs["L0_ffn_down_packed"]).T, 8, axis=0)
    pre["gusR"] = np.repeat(np.asarray(inputs["L0_ffn_gate_up_scales"])
                            .astype(np.float16, copy=False), 128, axis=0)
    pre["dnsR"] = np.repeat(np.asarray(inputs["L0_ffn_down_scales"])
                            .astype(np.float16, copy=False), 128, axis=0)
    return pre


_CACHE = {}


def kernel(**inputs):
    if "nc" not in _CACHE:
        _CACHE["nc"] = Builder(debug=False).build()
    nc = _CACHE["nc"]
    pre = prep_all(inputs)
    in_maps = [_prep_core(pre, h) for h in range(8)]
    res = run_bass_kernel_spmd(nc, in_maps, core_ids=list(range(8)))
    out = np.concatenate([res.results[h]["out_c"] for h in range(8)], axis=0)
    return out.reshape(1, SEQ, DM).astype(np.float32)
